# version 02 (final)
# baseline (speedup 1.0000x reference)
import numpy as np
import ml_dtypes
import concourse.bass as bass
import concourse.mybir as mybir
from concourse.bass_utils import run_bass_kernel_spmd

F32 = mybir.dt.float32
BF16 = mybir.dt.bfloat16
AF = mybir.ActivationFunctionType
OP = mybir.AluOpType
AX = mybir.AxisListType

NSEQ, T, D = 2, 2048, 1024
NTOK = NSEQ * T
TAU = 512
NTAU = NTOK // TAU
ND = 8
H, HD = 16, 64
FF = 4096
NFF = 32
NE = 8
MI = 512
NMI = 4
EPS_OUT = 1e-5
EPS_IN = 1e-6
DEBUG = False


class Buf:
    __slots__ = ("w", "rs", "name")

    def __init__(self, name=""):
        self.w = None
        self.rs = []
        self.name = name


class Em:
    ENGS = ("T", "V", "S", "G", "Y")

    def __init__(self):
        self.items = {e: [] for e in self.ENGS}
        self.cnt = {e: 0 for e in self.ENGS}
        self.last_wait = {e: {} for e in self.ENGS}

    def _dep(self, eng, se, val):
        if se == eng and eng in ("T", "G", "Y"):
            return
        lw = self.last_wait[eng]
        if lw.get(se, 0) >= val:
            return
        lw[se] = val
        self.items[eng].append(("w", se, val))

    def op(self, eng, fn, reads=(), writes=(), deps=()):
        dl = list(deps)
        for b in reads:
            if b.w is not None:
                dl.append(b.w)
        for b in writes:
            if b.w is not None:
                dl.append(b.w)
            dl.extend(b.rs)
        best = {}
        for se, val in dl:
            if best.get(se, 0) < val:
                best[se] = val
        for se in sorted(best):
            self._dep(eng, se, best[se])
        amount = 16 if eng in ("G", "Y") else 1
        self.cnt[eng] += amount
        self.items[eng].append(("c", fn, amount))
        tok = (eng, self.cnt[eng])
        for b in reads:
            b.rs.append(tok)
        for b in writes:
            b.w = tok
            b.rs = []
        return tok

    def dma(self, eng, out_ap, in_ap, reads=(), writes=(), deps=()):
        return self.op(eng, lambda e: e.dma_start(out=out_ap, in_=in_ap),
                       reads=reads, writes=writes, deps=deps)

    def replay(self, eng, e, sems):
        for it in self.items[eng]:
            if it[0] == "w":
                e.wait_ge(sems[it[1]], it[2])
            else:
                it[1](e).then_inc(sems[eng], it[2])


_EM = None


def build_program():
    global _EM
    nc = bass.Bass("TRN2", target_bir_lowering=False)
    em = Em()

    dt = {}

    def din(name, shape, dtype=F32):
        dt[name] = nc.dram_tensor(name, shape, dtype, kind="ExternalInput").ap()
        return dt[name]

    x_in = din("x", [NTOK, D])
    for nm in ("aWq", "aWk", "aWv", "aWo"):
        din(nm, [D, D])
    for nm in ("abq", "abk", "abv", "abo"):
        din(nm, [D])
    for nm in ("bWq", "bWk", "bWv", "bWo"):
        din(nm, [4, D, D])
    for nm in ("bbq", "bbk", "bbv", "bbo", "bln1s", "bln1b", "bln2s", "bln2b", "bb2"):
        din(nm, [4, D])
    din("bW1", [4, D, FF])
    din("bb1", [4, FF])
    din("bW2", [4, FF, D])
    din("ggw", [NE, D])
    din("gwg", [NE, MI, D])
    din("gwu", [NE, MI, D])
    din("gwd", [NE, D, MI])
    din("gsg", [MI, D])
    din("gsu", [MI, D])
    din("gsd", [D, MI])
    din("idf", [128, 128], F32)
    din("ones_col", [128, 1], BF16)
    din("ones_row", [1, 128], F32)
    din("sel16", [16, H * HD], BF16)
    din("sel8", [8, NE * 128], BF16)

    y_out = nc.dram_tensor("y", [NTOK, D], F32, kind="ExternalOutput").ap()

    def dint(name, shape, dtype):
        return nc.dram_tensor(name, shape, dtype, kind="Internal").ap()

    xs = dint("xs", [NTAU, 128, ND * TAU], F32)
    xbs = dint("xbs", [NTAU, 128, ND * TAU], F32)
    qsc = dint("qsc", [NTAU, 128, ND * TAU], BF16)
    lns = dint("lns", [NTAU, 128, ND * TAU], BF16)
    hsc = dint("hsc", [NE + 1, NTAU, 128, NMI * TAU], BF16)
    mtg = dint("mtg", [NE + 1, 128, ND * 512], BF16)
    mtu = dint("mtu", [NE + 1, 128, ND * 512], BF16)
    mtd = dint("mtd", [NE + 1, 128, NMI * 1024], BF16)
    dbg = {}
    if DEBUG:
        for k in range(6):
            dbg[k] = nc.dram_tensor(f"dbg{k}", [NTAU, 128, ND * TAU], F32,
                                    kind="ExternalOutput").ap()
        dbg["ln"] = nc.dram_tensor("dbgln", [NTAU, 128, ND * TAU], BF16,
                                   kind="ExternalOutput").ap()
        dbg["cw"] = nc.dram_tensor("dbgcw", [8, NTAU * TAU], BF16,
                                   kind="ExternalOutput").ap()
        dbg["h8"] = nc.dram_tensor("dbgh8", [NTAU, 128, NMI * TAU], BF16,
                                   kind="ExternalOutput").ap()
        dbg["h0"] = nc.dram_tensor("dbgh0", [NTAU, 128, NMI * TAU], BF16,
                                   kind="ExternalOutput").ap()
        dbg["lg"] = nc.dram_tensor("dbglg", [NTAU, 8, TAU], F32,
                                   kind="ExternalOutput").ap()
        dbg["sm"] = nc.dram_tensor("dbgsm", [NTAU, 128, 96], F32,
                                   kind="ExternalOutput").ap()
        dbg["sa"] = nc.dram_tensor("dbgsa", [NTAU, 128, 20], F32,
                                   kind="ExternalOutput").ap()
    qscr = dint("qscr", [4, 3, ND, ND, 128, 128], BF16)
    oscr = dint("oscr", [4, ND, ND, 128, 128], BF16)
    w1scr = dint("w1scr", [4, NFF, ND, 128, 128], BF16)
    w2scr = dint("w2scr", [4, ND, NFF, 128, 128], BF16)

    ar = nc.auto_arena()
    sb = ar.sbuf_tensor

    c_idf = sb("c_idf", [128, 128], F32)
    c_ones_col = sb("c_ones_col", [128, 1], BF16)
    c_ones_row = sb("c_ones_row", [1, 128], F32)
    c_sel16 = sb("c_sel16", [16, H * HD], BF16)
    c_sel8 = sb("c_sel8", [8, NE * 128], BF16)
    rows = sb("rows", [1, 2 * TAU], F32)
    rows2 = sb("rows2", [1, 2 * TAU], F32)
    pcol = sb("pcol", [128, 32], F32)
    pcol2 = sb("pcol2", [128, 104], F32)
    xr0 = sb("xr0", [128, ND * TAU], F32)
    xr1 = sb("xr1", [128, ND * TAU], F32)
    ua = sb("ua", [128, ND * TAU], F32)
    ub = sb("ub", [128, ND * TAU], F32)
    lb0 = sb("lb0", [128, ND * TAU], BF16)
    lb1 = sb("lb1", [128, ND * TAU], BF16)
    lb3 = sb("lb3", [128, ND * TAU], BF16)
    sa = sb("sa", [128, TAU], F32)
    sc_ = sb("sc", [128, TAU], F32)
    sd_ = sb("sd", [128, TAU], F32)
    se_ = sb("se", [128, TAU], F32)
    wt0 = sb("wt0", [128, 2048], F32)
    wt1 = sb("wt1", [128, 2048], F32)
    lgT = sb("lgT", [8, TAU], F32)
    rsm = sb("rsm", [128, 96], F32)
    rb16 = sb("rb16", [16, TAU], BF16)
    gateT = sb("gateT", [128, ND * 8], BF16)
    akvb = [sb(f"akvb{s}", [128, 8 * 65], BF16) for s in range(NSEQ)]
    akt = [sb(f"akt{s}", [128, ND * 16], BF16) for s in range(NSEQ)]
    mark = ar.addr

    wqT = sb("wqT", [128, ND * 1024], BF16)
    wkT = sb("wkT", [128, ND * 1024], BF16)
    wvT = sb("wvT", [128, ND * 1024], BF16)
    ktT = sb("ktT", [128, 4 * 1024], BF16)
    vtT = sb("vtT", [128, 4 * 1024], BF16)
    akc0 = sb("akc0", [128, 8 * 65], F32)
    akc1 = sb("akc1", [128, 8 * 65], F32)
    phA = ar.addr

    ar.addr = mark
    weg0 = sb("weg0", [128, ND * 512], BF16)
    weu0 = sb("weu0", [128, ND * 512], BF16)
    weg1 = sb("weg1", [128, ND * 512], BF16)
    weu1 = sb("weu1", [128, ND * 512], BF16)
    cw_all = sb("cw_all", [8, NTAU * TAU], BF16)
    wed0 = sb("wed0", [128, NMI * 1024], BF16)
    wed1 = sb("wed1", [128, NMI * 1024], BF16)
    hh0 = sb("hh0", [128, NMI * TAU], BF16)
    hh1 = sb("hh1", [128, NMI * TAU], BF16)
    phM = ar.addr

    ar.addr = mark
    hfb = sb("hfb", [128, NFF * TAU], BF16)
    kvacc = sb("kvacc", [128, ND * (NTAU + 1)], F32)
    ksacc = sb("ksacc", [128, ND * (NTAU + 1)], F32)
    wsq0 = sb("wsq0", [128, 1024], BF16)
    wsq1 = sb("wsq1", [128, 1024], BF16)
    wsq2 = sb("wsq2", [128, 1024], BF16)
    wsq3 = sb("wsq3", [128, 1024], BF16)
    wsw0 = sb("wsw0", [128, 4096], BF16)
    wsw1 = sb("wsw1", [128, 4096], BF16)
    phB = ar.addr

    ar.addr = max(phA, phM, phB)
    ar.allocate()
    assert ar.size <= 206000, f"sbuf overflow {ar.size}"

    wt0b = wt0[:].bitcast(BF16)
    wt1b = wt1[:].bitcast(BF16)

    with (
        nc.psum_tensor("pb0", [128, 512], F32) as pb0,
        nc.psum_tensor("pb1", [128, 512], F32) as pb1,
        nc.psum_tensor("pb2", [128, 512], F32) as pb2,
        nc.psum_tensor("pb3", [128, 512], F32) as pb3,
        nc.psum_tensor("pb4", [128, 512], F32) as pb4,
        nc.psum_tensor("pb5", [128, 512], F32) as pb5,
        nc.psum_tensor("pb6", [128, 512], F32) as pb6,
        nc.psum_tensor("pb7", [128, 512], F32) as pb7,
        nc.semaphore("smT") as smT,
        nc.semaphore("smV") as smV,
        nc.semaphore("smS") as smS,
        nc.semaphore("smG") as smG,
        nc.semaphore("smY") as smY,
        nc.Block() as block,
    ):
        sems = {"T": smT, "V": smV, "S": smS, "G": smG, "Y": smY}
        PB = [pb0, pb1, pb2, pb3, pb4, pb5, pb6, pb7]
        pbf = [Buf(f"pb{i}") for i in range(8)]

        names = dict(rows=rows, rows2=rows2, pcol=pcol, pcol2=pcol2, xr0=xr0, xr1=xr1,
                     ua=ua, ub=ub, lb0=lb0, lb1=lb1, lb3=lb3, sa=sa, sc_=sc_, sd_=sd_,
                     se_=se_, wt0=wt0, wt1=wt1, lgT=lgT, cw_all=cw_all, rsm=rsm,
                     rb16=rb16, kvacc=kvacc, ksacc=ksacc, akc0=akc0, akc1=akc1,
                     wqT=wqT, wkT=wkT, wvT=wvT, gateT=gateT, ktT=ktT, vtT=vtT,
                     weg0=weg0, weu0=weu0, weg1=weg1, weu1=weu1,
                     wed0=wed0, wed1=wed1, hh0=hh0, hh1=hh1,
                     hfb=hfb, wsq0=wsq0, wsq1=wsq1, wsq2=wsq2, wsq3=wsq3,
                     wsw0=wsw0, wsw1=wsw1)
        B = {k: Buf(k) for k in names}
        for s_ in range(NSEQ):
            B[f"akvb{s_}"] = Buf(f"akvb{s_}")
            B[f"akt{s_}"] = Buf(f"akt{s_}")
        CB = Buf("consts")
        xsB = [Buf(f"xs{i}") for i in range(NTAU)]
        xbB = [Buf(f"xb{i}") for i in range(NTAU)]
        qsB = [Buf(f"qs{i}") for i in range(NTAU)]
        lnB_ = [Buf(f"lns{i}") for i in range(NTAU)]
        hsB = [[Buf(f"hs{e}_{t}") for t in range(NTAU)] for e in range(NE + 1)]
        mtB = [Buf(f"mt{e}") for e in range(NE + 1)]
        scrB = {"q": [Buf(f"qscr{i}") for i in range(4)],
                "o": [Buf(f"oscr{i}") for i in range(4)],
                "w1": [Buf(f"w1scr{i}") for i in range(4)],
                "w2": [Buf(f"w2scr{i}") for i in range(4)]}
        yB = Buf("y")

        def mm(bank_ap, lhsT_ap, rhs_ap, start, stop):
            return lambda e: e.matmul(bank_ap, lhsT_ap, rhs_ap, start=start, stop=stop)

        em.dma("Y", c_idf[:], dt["idf"], writes=(CB,))
        em.dma("Y", c_ones_col[:], dt["ones_col"], writes=(CB,))
        em.dma("Y", c_ones_row[:], dt["ones_row"], writes=(CB,))
        em.dma("Y", c_sel16[:], dt["sel16"], writes=(CB,))
        em.dma("Y", c_sel8[:], dt["sel8"], writes=(CB,))

        def col_param(row_ap, n, dst_t, dst_B, col0, bank):
            stB = B["wt1"]
            for c0 in range(0, n, 16):
                cn = min(16, n - c0)
                em.dma("Y", wt1[0:1, 0:cn * 128], row_ap[None, c0 * 128:(c0 + cn) * 128], writes=(stB,))

                def tf(e, cn=cn):
                    ins = None
                    for i in range(cn):
                        ins = e.transpose(PB[bank][:, i:i + 1], wt1[0:1, i * 128:(i + 1) * 128],
                                          c_idf[0:1, 0:1])
                    return ins
                em.op("T", tf, reads=(stB, CB), writes=(pbf[bank],))
                em.op("S", (lambda e, c0=c0, cn=cn: e.copy(dst_t[:, col0 + c0:col0 + c0 + cn],
                                                           PB[bank][:, 0:cn])),
                      reads=(pbf[bank],), writes=(dst_B,))

        def prep_T(dst_view_fn, src_ap, nrow, ncol, banks, store_fn=None):
            for ri in range(nrow):
                wtt, wtB = (wt0, B["wt0"]) if ri % 2 == 0 else (wt1, B["wt1"])
                ncol_el = ncol * 128
                nh = (ncol_el + 2047) // 2048
                for hh in range(nh):
                    c0 = hh * 2048
                    c1 = min(ncol_el, c0 + 2048)
                    em.dma("Y", wtt[:, 0:c1 - c0], src_ap[ri * 128:(ri + 1) * 128, c0:c1], writes=(wtB,))
                    for cj in range(c0 // 128, c1 // 128):
                        bk = banks[cj % len(banks)]
                        em.op("T", (lambda e, cj=cj, bk=bk, c0=c0, wtt=wtt: e.transpose(
                            PB[bk][:, 0:128], wtt[:, cj * 128 - c0:(cj + 1) * 128 - c0], c_idf[:])),
                            reads=(wtB, CB), writes=(pbf[bk],))
                        dap, dbuf = dst_view_fn(ri, cj)
                        em.op("S", (lambda e, dap=dap, bk=bk: e.copy(dap, PB[bk][:, 0:128])),
                              reads=(pbf[bk],), writes=(dbuf,))

        def awt_dst(dst_t, dst_B):
            def f(ri, cj):
                return dst_t[:, cj * 1024 + ri * 128: cj * 1024 + (ri + 1) * 128], dst_B
            return f

        prep_T(awt_dst(wqT, B["wqT"]), dt["aWq"], ND, ND, [0, 1])
        prep_T(awt_dst(wkT, B["wkT"]), dt["aWk"], ND, ND, [0, 1])
        prep_T(awt_dst(wvT, B["wvT"]), dt["aWv"], ND, ND, [0, 1])

        em.dma("Y", wt0[0:8, 0:1024], dt["ggw"], writes=(B["wt0"],))
        for cj in range(ND):
            bk = [2, 3][cj % 2]
            em.op("T", (lambda e, cj=cj, bk=bk: e.transpose(
                PB[bk][:, 0:8], wt0[0:8, cj * 128:(cj + 1) * 128], c_idf[0:8, 0:8])),
                reads=(B["wt0"], CB), writes=(pbf[bk],))
            em.op("S", (lambda e, cj=cj, bk=bk: e.copy(gateT[:, cj * 8:(cj + 1) * 8], PB[bk][:, 0:8])),
                  reads=(pbf[bk],), writes=(B["gateT"],))

        col_param(dt["abq"], ND, pcol, B["pcol"], 0, 4)
        col_param(dt["abk"], ND, pcol, B["pcol"], 8, 4)
        col_param(dt["abv"], ND, pcol, B["pcol"], 16, 4)
        col_param(dt["abo"], ND, pcol, B["pcol"], 24, 4)

        def ln_full(src_t, src_B, eps, out_bf=None, out_f32=None, affine=None,
                    banks=(6, 7), tag=""):
            ba, bb = banks
            em.op("S", lambda e: e.copy(lb3[:], src_t[:]), reads=(src_B,), writes=(B["lb3"],))
            sqv = lb0 if out_bf is None or out_bf[0] is not lb0 else lb1
            sqB = B["lb0"] if sqv is lb0 else B["lb1"]
            em.op("V", lambda e: e.tensor_tensor(sqv[:], lb3[:], lb3[:], OP.mult),
                  reads=(B["lb3"],), writes=(sqB,))

            def statf(e, srcv, bk):
                ins = None
                for j in range(ND):
                    ins = e.matmul(PB[bk][0:1, :], c_ones_col[:], srcv[:, j * TAU:(j + 1) * TAU],
                                   start=(j == 0), stop=(j == ND - 1))
                return ins
            em.op("T", lambda e: statf(e, lb3, ba), reads=(B["lb3"], CB), writes=(pbf[ba],))
            em.op("T", lambda e: statf(e, sqv, bb), reads=(sqB, CB), writes=(pbf[bb],))
            em.op("V", lambda e: e.tensor_scalar(rows[0:1, 0:TAU], PB[ba][0:1, :], 1.0 / D, None, OP.mult),
                  reads=(pbf[ba],), writes=(B["rows"],))
            em.op("V", lambda e: e.tensor_tensor(rows2[0:1, TAU:2 * TAU], rows[0:1, 0:TAU],
                                                 rows[0:1, 0:TAU], OP.mult),
                  reads=(B["rows"],), writes=(B["rows2"],))
            em.op("V", lambda e: e.scalar_tensor_tensor(rows2[0:1, 0:TAU], PB[bb][0:1, :],
                                                        1.0 / D, rows2[0:1, TAU:2 * TAU],
                                                        OP.mult, OP.subtract),
                  reads=(pbf[bb], B["rows2"]), writes=(B["rows2"],))
            em.op("V", lambda e: e.tensor_scalar(rows[0:1, TAU:2 * TAU], rows2[0:1, 0:TAU],
                                                 float(eps), None, OP.add),
                  reads=(B["rows2"],), writes=(B["rows"],))
            em.op("S", lambda e: e.activation(rows2[0:1, TAU:2 * TAU], rows[0:1, TAU:2 * TAU],
                                              AF.Ln),
                  reads=(B["rows"],), writes=(B["rows2"],))
            em.op("S", lambda e: e.activation(rows[0:1, TAU:2 * TAU], rows2[0:1, TAU:2 * TAU],
                                              AF.Exp, scale=-0.5),
                  reads=(B["rows2"],), writes=(B["rows"],))
            em.op("V", lambda e: e.scalar_tensor_tensor(rows2[0:1, 0:TAU], rows[0:1, 0:TAU],
                                                        -1.0, rows[0:1, TAU:2 * TAU], OP.mult, OP.mult),
                  reads=(B["rows"], B["rows2"]), writes=(B["rows2"],))
            em.op("T", mm(PB[ba][:, :], c_ones_row[:], rows[0:1, TAU:2 * TAU], True, True),
                  reads=(B["rows"], CB), writes=(pbf[ba],))
            em.op("T", mm(PB[bb][:, :], c_ones_row[:], rows2[0:1, 0:TAU], True, True),
                  reads=(B["rows2"], CB), writes=(pbf[bb],))
            em.op("V", lambda e: e.tensor_tensor(
                ub[:].rearrange("p (j t) -> p j t", j=ND),
                src_t[:].rearrange("p (j t) -> p j t", j=ND),
                PB[ba][:, None, :].broadcast_to([128, ND, TAU]), OP.mult),
                reads=(src_B, pbf[ba]), writes=(B["ub"],))
            if affine is None:
                outs = []
                if out_f32 is not None:
                    em.op("V", lambda e: e.tensor_tensor(
                        out_f32[0][:].rearrange("p (j t) -> p j t", j=ND),
                        ub[:].rearrange("p (j t) -> p j t", j=ND),
                        PB[bb][:, None, :].broadcast_to([128, ND, TAU]), OP.add),
                        reads=(B["ub"], pbf[bb]), writes=(out_f32[1],))
                    if out_bf is not None:
                        em.op("S", lambda e: e.copy(out_bf[0][:], out_f32[0][:]),
                              reads=(out_f32[1],), writes=(out_bf[1],))
                else:
                    em.op("V", lambda e: e.tensor_tensor(
                        out_bf[0][:].rearrange("p (j t) -> p j t", j=ND),
                        ub[:].rearrange("p (j t) -> p j t", j=ND),
                        PB[bb][:, None, :].broadcast_to([128, ND, TAU]), OP.add),
                        reads=(B["ub"], pbf[bb]), writes=(out_bf[1],))
                return
            em.op("V", lambda e: e.tensor_tensor(
                ua[:].rearrange("p (j t) -> p j t", j=ND),
                ub[:].rearrange("p (j t) -> p j t", j=ND),
                PB[bb][:, None, :].broadcast_to([128, ND, TAU]), OP.add),
                reads=(B["ub"], pbf[bb]), writes=(B["ua"],))
            s_t, sB, s_col0, b_col0 = affine
            for j in range(ND):
                em.op("V", (lambda e, j=j: e.tensor_scalar(
                    out_f32[0][:, j * TAU:(j + 1) * TAU], ua[:, j * TAU:(j + 1) * TAU],
                    s_t[:, s_col0 + j:s_col0 + j + 1], s_t[:, b_col0 + j:b_col0 + j + 1],
                    OP.mult, OP.add)),
                    reads=(B["ua"], sB), writes=(out_f32[1],))
            if out_bf is not None:
                em.op("S", lambda e: e.copy(out_bf[0][:], out_f32[0][:]),
                      reads=(out_f32[1],), writes=(out_bf[1],))

        def convert_in(tau, xr_t, xr_B):
            wtt, wtB = (wt0, B["wt0"]) if tau % 2 == 0 else (wt1, B["wt1"])
            for ti in range(4):
                t0 = tau * TAU + ti * 128
                em.dma("G", wtt[:, 0:1024], x_in[t0:t0 + 128, :], writes=(wtB,))
                for cj in range(ND):
                    bk = [4, 5][cj % 2]
                    em.op("T", (lambda e, cj=cj, bk=bk, wtt=wtt: e.transpose(
                        PB[bk][:, 0:128], wtt[:, cj * 128:(cj + 1) * 128], c_idf[:])),
                        reads=(wtB, CB), writes=(pbf[bk],))
                    em.op("S", (lambda e, cj=cj, ti=ti, bk=bk: e.copy(
                        xr_t[:, cj * TAU + ti * 128:cj * TAU + (ti + 1) * 128], PB[bk][:, 0:128])),
                        reads=(pbf[bk],), writes=(xr_B,))
            em.dma("G", xs[tau], xr_t[:], reads=(xr_B,), writes=(xsB[tau],))

        def elu1(bank, bias_t, bias_B, bcol, out_emit):
            em.op("V", (lambda e: e.tensor_scalar(sa[:], PB[bank][:, :],
                                                  bias_t[:, bcol:bcol + 1], 0.0, OP.add, OP.min)),
                  reads=(pbf[bank], bias_B), writes=(B["sa"],))
            em.op("S", lambda e: e.activation(sc_[:], sa[:], AF.Exp),
                  reads=(B["sa"],), writes=(B["sc_"],))
            em.op("S", (lambda e: e.activation(sd_[:], PB[bank][:, :], AF.Relu,
                                               bias=bias_t[:, bcol:bcol + 1])),
                  reads=(pbf[bank], bias_B), writes=(B["sd_"],))
            out_emit()

        def attn_a_pass1(s, tl):
            tau = s * 4 + tl
            if tl == 0:
                em.op("V", lambda e: e.memset(akc0[:], 0.0), writes=(B["akc0"],))
            xr_t, xr_B = (xr0, B["xr0"]) if tl % 2 == 0 else (xr1, B["xr1"])
            convert_in(tau, xr_t, xr_B)
            ln_full(xr_t, xr_B, EPS_OUT, out_bf=(lb1, B["lb1"]))
            for wi, (wT, wTB, bcol) in enumerate(((wqT, B["wqT"], 0), (wkT, B["wkT"], 8),
                                                  (wvT, B["wvT"], 16))):
                for m in range(ND):
                    bk = [0, 1][m % 2]

                    def pf(e, wT=wT, m=m, bk=bk):
                        ins = None
                        for k in range(ND):
                            ins = e.matmul(PB[bk][:, :],
                                           wT[:, k * 1024 + m * 128:k * 1024 + (m + 1) * 128],
                                           lb1[:, k * TAU:(k + 1) * TAU],
                                           start=(k == 0), stop=(k == ND - 1))
                        return ins
                    em.op("T", pf, reads=(wTB, B["lb1"]), writes=(pbf[bk],))
                    if wi == 0:
                        def oq(m=m):
                            em.op("V", (lambda e, m=m: e.tensor_tensor(
                                lb0[:, m * TAU:(m + 1) * TAU], sc_[:], sd_[:], OP.add)),
                                reads=(B["sc_"], B["sd_"]), writes=(B["lb0"],))
                        elu1(bk, pcol, B["pcol"], 0 + m, oq)
                    elif wi == 1:
                        def ok(m=m):
                            em.op("V", (lambda e, m=m: e.tensor_tensor(
                                ua[:, m * TAU:(m + 1) * TAU], sc_[:], sd_[:], OP.add)),
                                reads=(B["sc_"], B["sd_"]), writes=(B["ua"],))
                        elu1(bk, pcol, B["pcol"], 8 + m, ok)
                    else:
                        em.op("V", (lambda e, m=m, bk=bk: e.tensor_scalar(
                            ub[:, m * TAU:(m + 1) * TAU], PB[bk][:, :],
                            pcol[:, 16 + m:16 + m + 1], None, OP.add)),
                            reads=(pbf[bk], B["pcol"]), writes=(B["ub"],))
            em.dma("G", qsc[tau], lb0[:], reads=(B["lb0"],), writes=(qsB[tau],))
            for src_t, src_B, dstv, dstB in ((ua, B["ua"], ktT, B["ktT"]),
                                             (ub, B["ub"], vtT, B["vtT"])):
                for ti in range(4):
                    for cj in range(ND):
                        bk = [2, 3][cj % 2]
                        em.op("T", (lambda e, cj=cj, ti=ti, bk=bk, src_t=src_t: e.transpose(
                            PB[bk][:, 0:128], src_t[:, cj * TAU + ti * 128:cj * TAU + (ti + 1) * 128],
                            c_idf[:])),
                            reads=(src_B, CB), writes=(pbf[bk],))
                        em.op("S", (lambda e, cj=cj, ti=ti, bk=bk, dstv=dstv: e.copy(
                            dstv[:, ti * 1024 + cj * 128:ti * 1024 + (cj + 1) * 128],
                            PB[bk][:, 0:128])),
                            reads=(pbf[bk],), writes=(dstB,))
            for h in range(H):
                bk = [4, 5][h % 2]

                def kvf(e, h=h, bk=bk):
                    ins = None
                    for ti in range(4):
                        ins = e.matmul(PB[bk][0:64, 0:64],
                                       ktT[:, ti * 1024 + h * 64:ti * 1024 + (h + 1) * 64],
                                       vtT[:, ti * 1024 + h * 64:ti * 1024 + (h + 1) * 64],
                                       start=(ti == 0), stop=False)
                    for ti in range(4):
                        ins = e.matmul(PB[bk][0:64, 64:65],
                                       ktT[:, ti * 1024 + h * 64:ti * 1024 + (h + 1) * 64],
                                       c_ones_col[:], start=(ti == 0), stop=(ti == 3))
                    return ins
                em.op("T", kvf, reads=(B["ktT"], B["vtT"], CB), writes=(pbf[bk],))
                r0 = (h % 2) * 64
                hp = h // 2
                rd_t, rd_B = (akc0, B["akc0"]) if tl % 2 == 0 else (akc1, B["akc1"])
                wr_t, wr_B = (akc1, B["akc1"]) if tl % 2 == 0 else (akc0, B["akc0"])
                em.op("V", (lambda e, r0=r0, hp=hp, bk=bk, rd_t=rd_t, wr_t=wr_t: e.tensor_tensor(
                    wr_t[r0:r0 + 64, hp * 65:(hp + 1) * 65],
                    rd_t[r0:r0 + 64, hp * 65:(hp + 1) * 65],
                    PB[bk][0:64, 0:65], OP.add)),
                    reads=(pbf[bk], rd_B), writes=(wr_B,))

        def attn_a_finalize(s):
            em.op("S", (lambda e, s=s: e.copy(akvb[s][:], akc0[:])),
                  reads=(B["akc0"],), writes=(B[f"akvb{s}"],))
            em.op("V", (lambda e, s=s: e.memset(akt[s][:], 0.0)), writes=(B[f"akt{s}"],))
            for h in range(H):
                r0 = (h % 2) * 64
                c0 = (h // 2) * 65 + 64
                kj = h // 2
                em.op("S", (lambda e, s=s, r0=r0, c0=c0, kj=kj, h=h: e.copy(
                    akt[s][r0:r0 + 64, kj * 16 + h:kj * 16 + h + 1], akc0[r0:r0 + 64, c0:c0 + 1])),
                    reads=(B["akc0"],), writes=(B[f"akt{s}"],))

        def attn_a_phase3(s, tl):
            tau = s * 4 + tl
            em.dma("G", lb0[:], qsc[tau], reads=(qsB[tau],), writes=(B["lb0"],))
            xr_t, xr_B = (xr0, B["xr0"]) if tl % 2 == 0 else (xr1, B["xr1"])
            em.dma("G", xr_t[:], xs[tau], reads=(xsB[tau],), writes=(xr_B,))
            bkD = 6

            def df(e, s=s):
                ins = None
                for kj in range(ND):
                    ins = e.matmul(PB[bkD][0:16, :], akt[s][:, kj * 16:(kj + 1) * 16],
                                   lb0[:, kj * TAU:(kj + 1) * TAU],
                                   start=(kj == 0), stop=(kj == ND - 1))
                return ins
            em.op("T", df, reads=(B[f"akt{s}"], B["lb0"]), writes=(pbf[bkD],))
            em.op("V", lambda e: e.tensor_scalar(rsm[0:16, 0:64].rearrange("p (a b) -> p a b", a=1)
                                                 if False else rsm[0:16, 0:64],
                                                 PB[bkD][0:16, 0:64], 1e-6, None, OP.max),
                  reads=(pbf[bkD],), writes=(B["rsm"],)) if False else None
            em.op("V", lambda e: e.tensor_scalar(sa[0:16, :], PB[bkD][0:16, :], 1e-6, None, OP.max),
                  reads=(pbf[bkD],), writes=(B["sa"],))
            em.op("S", lambda e: e.activation(sc_[0:16, :], sa[0:16, :], AF.Ln),
                  reads=(B["sa"],), writes=(B["sc_"],))
            em.op("S", lambda e: e.activation(sd_[0:16, :], sc_[0:16, :], AF.Exp, scale=-1.0),
                  reads=(B["sc_"],), writes=(B["sd_"],))
            em.op("S", lambda e: e.copy(rb16[:], sd_[0:16, :]), reads=(B["sd_"],), writes=(B["rb16"],))
            for h in range(H):
                bk = [4, 5][h % 2]
                bk2 = [0, 1][h % 2]
                em.op("T", mm(PB[bk][0:64, :], c_sel16[:, h * 64:(h + 1) * 64], rb16[:], True, True),
                      reads=(CB, B["rb16"]), writes=(pbf[bk],))
                r0 = (h % 2) * 64
                mq = h // 2
                em.op("V", (lambda e, h=h, bk=bk, r0=r0, mq=mq: e.tensor_tensor(
                    lb3[r0:r0 + 64, mq * TAU:(mq + 1) * TAU],
                    lb0[r0:r0 + 64, mq * TAU:(mq + 1) * TAU], PB[bk][0:64, :], OP.mult)),
                    reads=(B["lb0"], pbf[bk]), writes=(B["lb3"],))
                em.op("T", (lambda e, s=s, h=h, bk2=bk2, r0=r0, mq=mq: e.matmul(
                    PB[bk2][0:64, :], akvb[s][r0:r0 + 64, (h // 2) * 65:(h // 2) * 65 + 64],
                    lb3[r0:r0 + 64, mq * TAU:(mq + 1) * TAU], start=True, stop=True)),
                    reads=(B[f"akvb{s}"], B["lb3"]), writes=(pbf[bk2],))
                em.op("S", (lambda e, bk2=bk2, r0=r0, mq=mq: e.copy(
                    lb1[r0:r0 + 64, mq * TAU:(mq + 1) * TAU], PB[bk2][0:64, :])),
                    reads=(pbf[bk2],), writes=(B["lb1"],))
            for m in range(ND):
                bk = [2, 3][m % 2]

                def wof(e, m=m, bk=bk):
                    ins = None
                    for k in range(ND):
                        col = k * 1024 + m * 128
                        src = ktT if col < 4096 else vtT
                        cc = col if col < 4096 else col - 4096
                        ins = e.matmul(PB[bk][:, :], src[:, cc:cc + 128],
                                       lb1[:, k * TAU:(k + 1) * TAU],
                                       start=(k == 0), stop=(k == ND - 1))
                    return ins
                em.op("T", wof, reads=(B["ktT"], B["vtT"], B["lb1"]), writes=(pbf[bk],))
                em.op("V", (lambda e, m=m, bk=bk, xr_t=xr_t: e.scalar_tensor_tensor(
                    ub[:, m * TAU:(m + 1) * TAU], PB[bk][:, :], pcol[:, 24 + m:24 + m + 1],
                    xr_t[:, m * TAU:(m + 1) * TAU], OP.add, OP.add)),
                    reads=(pbf[bk], B["pcol"], xr_B), writes=(B["ub"],))
            em.dma("G", xs[tau], ub[:], reads=(B["ub"],), writes=(xsB[tau],))

        def moe_prep():
            for e_i in range(NE + 1):
                wg_src = dt["gwg"][e_i] if e_i < NE else dt["gsg"]
                wu_src = dt["gwu"][e_i] if e_i < NE else dt["gsu"]
                wd_src = dt["gwd"][e_i] if e_i < NE else dt["gsd"]
                for nm, src, dst in (("g", wg_src, mtg), ("u", wu_src, mtu)):
                    for ri in range(4):
                        em.dma("Y", wt0[:, 0:1024], src[ri * 128:(ri + 1) * 128, :], writes=(B["wt0"],))
                        for cj in range(ND):
                            bk = [6, 7][cj % 2]
                            em.op("T", (lambda e, cj=cj, bk=bk: e.transpose(
                                PB[bk][:, 0:128], wt0[:, cj * 128:(cj + 1) * 128], c_idf[:])),
                                reads=(B["wt0"], CB), writes=(pbf[bk],))
                            em.op("S", (lambda e, cj=cj, bk=bk: e.copy(
                                wt1b[:, cj * 128:(cj + 1) * 128], PB[bk][:, 0:128])),
                                reads=(pbf[bk],), writes=(B["wt1"],))
                        em.dma("G", dst[e_i].rearrange("p (j f) -> p j f", j=ND)[:, :, ri * 128:(ri + 1) * 128],
                               wt1b[:, 0:1024].rearrange("p (j f) -> p j f", j=ND),
                               reads=(B["wt1"],), writes=(mtB[e_i],))
                for ri in range(ND):
                    em.dma("Y", wt0[:, 0:512], wd_src[ri * 128:(ri + 1) * 128, :], writes=(B["wt0"],))
                    for cj in range(NMI):
                        bk = [6, 7][cj % 2]
                        em.op("T", (lambda e, cj=cj, bk=bk: e.transpose(
                            PB[bk][:, 0:128], wt0[:, cj * 128:(cj + 1) * 128], c_idf[:])),
                            reads=(B["wt0"], CB), writes=(pbf[bk],))
                        em.op("S", (lambda e, cj=cj, bk=bk: e.copy(
                            wt1b[:, cj * 128:(cj + 1) * 128], PB[bk][:, 0:128])),
                            reads=(pbf[bk],), writes=(B["wt1"],))
                    em.dma("G", mtd[e_i].rearrange("p (j f) -> p j f", j=NMI)[:, :, ri * 128:(ri + 1) * 128],
                           wt1b[:, 0:512].rearrange("p (j f) -> p j f", j=NMI),
                           reads=(B["wt1"],), writes=(mtB[e_i],))

        def barrier():
            return [(e, em.cnt[e]) for e in em.ENGS]

        def moe_phase(final, dep0=()):
            for tau in range(NTAU):
                xr_t, xr_B = (xr0, B["xr0"]) if tau % 2 == 0 else (xr1, B["xr1"])
                em.dma("G", xr_t[:], xs[tau], reads=(xsB[tau],), writes=(xr_B,))
                ln_full(xr_t, xr_B, EPS_OUT, out_bf=(lb1, B["lb1"]))
                em.dma("G", lns[tau], lb1[:], reads=(B["lb1"],), writes=(lnB_[tau],))
                bkL = 6

                def gf(e):
                    ins = None
                    for j in range(ND):
                        ins = e.matmul(PB[bkL][0:8, :], gateT[:, j * 8:(j + 1) * 8],
                                       lb1[:, j * TAU:(j + 1) * TAU],
                                       start=(j == 0), stop=(j == ND - 1))
                    return ins
                em.op("T", gf, reads=(B["gateT"], B["lb1"]), writes=(pbf[bkL],))
                em.op("S", lambda e: e.copy(lgT[:], PB[bkL][0:8, :]), reads=(pbf[bkL],), writes=(B["lgT"],))
                bkT = 7

                def tf(e):
                    ins = None
                    for ti in range(4):
                        ins = e.transpose(PB[bkT][:, ti * 8:(ti + 1) * 8],
                                          lgT[:, ti * 128:(ti + 1) * 128], c_idf[0:8, 0:8])
                    return ins
                em.op("T", tf, reads=(B["lgT"], CB), writes=(pbf[bkT],))
                em.op("V", lambda e: e.tensor_copy(rsm[:, 0:32], PB[bkT][:, 0:32]),
                      reads=(pbf[bkT],), writes=(B["rsm"],))
                Lv = rsm[:, 0:32].rearrange("p (a b) -> p a b", a=4)
                Av = rsm[:, 32:64].rearrange("p (a b) -> p a b", a=4)
                Bv = rsm[:, 64:96].rearrange("p (a b) -> p a b", a=4)

                def col(i):
                    return sa[:, 4 * i:4 * (i + 1)].rearrange("p (a b) -> p a b", b=1)

                def bcast_ts(dst0, src0, sacol, opx):
                    def f(e, dst0=dst0, src0=src0, sacol=sacol, opx=opx):
                        ins = None
                        for ti in range(4):
                            ins = e.tensor_scalar(
                                rsm[:, dst0 + ti * 8:dst0 + (ti + 1) * 8],
                                rsm[:, src0 + ti * 8:src0 + (ti + 1) * 8],
                                sa[:, sacol + ti:sacol + ti + 1], 0.0, opx, OP.add)
                        return ins
                    em.op("V", f, reads=(B["rsm"], B["sa"]), writes=(B["rsm"],))
                em.op("V", lambda e: e.tensor_reduce(col(0), Lv, AX.X, OP.max),
                      reads=(B["rsm"],), writes=(B["sa"],))
                bcast_ts(32, 0, 0, OP.subtract)
                em.op("S", lambda e: e.activation(rsm[:, 64:96], rsm[:, 32:64], AF.Exp),
                      reads=(B["rsm"],), writes=(B["rsm"],))
                em.op("V", lambda e: e.tensor_reduce(col(1), Bv, AX.X, OP.add),
                      reads=(B["rsm"],), writes=(B["sa"],))
                em.op("S", lambda e: e.activation(sa[:, 12:16], sa[:, 4:8], AF.Ln),
                      reads=(B["sa"],), writes=(B["sa"],))
                em.op("S", lambda e: e.activation(sa[:, 8:12], sa[:, 12:16], AF.Exp, scale=-1.0),
                      reads=(B["sa"],), writes=(B["sa"],))
                bcast_ts(32, 64, 8, OP.mult)
                em.op("V", lambda e: e.tensor_reduce(col(3), Av, AX.X, OP.max),
                      reads=(B["rsm"],), writes=(B["sa"],))
                bcast_ts(64, 32, 12, OP.is_ge)
                em.op("V", lambda e: e.scalar_tensor_tensor(Lv, Bv, -1e30, Av, OP.mult, OP.add),
                      reads=(B["rsm"],), writes=(B["rsm"],))
                em.op("V", lambda e: e.tensor_reduce(col(4), Lv, AX.X, OP.max),
                      reads=(B["rsm"],), writes=(B["sa"],))
                bcast_ts(64, 32, 16, OP.is_ge)
                em.op("V", lambda e: e.tensor_tensor(Lv, Av, Bv, OP.mult),
                      reads=(B["rsm"],), writes=(B["rsm"],))
                if DEBUG and final:
                    em.dma("Y", dbg["lg"][tau], lgT[:], reads=(B["lgT"],))
                    em.dma("Y", dbg["sm"][tau], rsm[:], reads=(B["rsm"],))
                    em.dma("Y", dbg["sa"][tau], sa[:, 0:20], reads=(B["sa"],))
                bkC = 6

                def cf(e):
                    ins = None
                    for ti in range(4):
                        ins = e.transpose(PB[bkC][0:8, ti * 128:(ti + 1) * 128],
                                          rsm[:, ti * 8:(ti + 1) * 8], c_idf[:])
                    return ins
                em.op("T", cf, reads=(B["rsm"], CB), writes=(pbf[bkC],))
                em.op("S", (lambda e, tau=tau: e.copy(cw_all[:, tau * TAU:(tau + 1) * TAU],
                                                      PB[bkC][0:8, :])),
                      reads=(pbf[bkC],), writes=(B["cw_all"],),
                      deps=(dep0 if tau == 0 else ()))

            if DEBUG and final:
                for tau in range(NTAU):
                    em.dma("Y", dbg["ln"][tau], lns[tau], reads=(lnB_[tau],))
                em.dma("Y", dbg["cw"], cw_all[:], reads=(B["cw_all"],))
            wslots = [(weg0, weu0, B["weg0"], B["weu0"]), (weg1, weu1, B["weg1"], B["weu1"])]
            for e_i in range(NE + 1):
                wg_t, wu_t, wgB, wuB = wslots[e_i % 2]
                em.dma("Y", wg_t[:], mtg[e_i], reads=(mtB[e_i],), writes=(wgB,),
                       deps=(dep0 if e_i == 0 else ()))
                em.dma("Y", wu_t[:], mtu[e_i], reads=(mtB[e_i],), writes=(wuB,))
                for tau in range(NTAU):
                    ln_t, lnBu = (lb0, B["lb0"]) if tau % 2 == 0 else (lb1, B["lb1"])
                    em.dma("G", ln_t[:], lns[tau], reads=(lnB_[tau],), writes=(lnBu,),
                           deps=(dep0 if (e_i == 0 and tau == 0) else ()))
                    ho_t, hoB = (hh0, B["hh0"]) if tau % 2 == 0 else (hh1, B["hh1"])
                    if e_i < NE:
                        bkW = 4
                        em.op("T", (lambda e, e_i=e_i, tau=tau: e.matmul(
                            PB[bkW][:, :], c_sel8[:, e_i * 128:(e_i + 1) * 128],
                            cw_all[:, tau * TAU:(tau + 1) * TAU], start=True, stop=True)),
                            reads=(CB, B["cw_all"]), writes=(pbf[bkW],))
                        em.op("S", lambda e: e.copy(se_[:], PB[bkW][:, :]),
                              reads=(pbf[bkW],), writes=(B["se_"],))
                    for fm in range(NMI):
                        bg = [0, 1][fm % 2]
                        bu = [2, 3][fm % 2]

                        def gmm(e, w=wg_t, fm=fm, bk=bg, ln_t=ln_t):
                            ins = None
                            for k in range(ND):
                                ins = e.matmul(PB[bk][:, :],
                                               w[:, k * 512 + fm * 128:k * 512 + (fm + 1) * 128],
                                               ln_t[:, k * TAU:(k + 1) * TAU],
                                               start=(k == 0), stop=(k == ND - 1))
                            return ins
                        em.op("T", gmm, reads=(wgB, lnBu), writes=(pbf[bg],))

                        def umm(e, w=wu_t, fm=fm, bk=bu, ln_t=ln_t):
                            ins = None
                            for k in range(ND):
                                ins = e.matmul(PB[bk][:, :],
                                               w[:, k * 512 + fm * 128:k * 512 + (fm + 1) * 128],
                                               ln_t[:, k * TAU:(k + 1) * TAU],
                                               start=(k == 0), stop=(k == ND - 1))
                            return ins
                        em.op("T", umm, reads=(wuB, lnBu), writes=(pbf[bu],))
                        em.op("S", (lambda e, bg=bg: e.activation(sa[:], PB[bg][:, :], AF.Silu)),
                              reads=(pbf[bg],), writes=(B["sa"],))
                        em.op("V", (lambda e, bu=bu: e.tensor_tensor(sc_[:], sa[:], PB[bu][:, :],
                                                                     OP.mult)),
                              reads=(B["sa"], pbf[bu]), writes=(B["sc_"],))
                        if e_i < NE:
                            em.op("V", (lambda e, fm=fm, ho_t=ho_t: e.tensor_tensor(
                                ho_t[:, fm * TAU:(fm + 1) * TAU], sc_[:], se_[:], OP.mult)),
                                reads=(B["sc_"], B["se_"]), writes=(hoB,))
                        else:
                            em.op("V", (lambda e, fm=fm, ho_t=ho_t: e.tensor_copy(
                                ho_t[:, fm * TAU:(fm + 1) * TAU], sc_[:])),
                                reads=(B["sc_"],), writes=(hoB,))
                    em.dma("G", hsc[e_i, tau], ho_t[:], reads=(hoB,), writes=(hsB[e_i][tau],))

            if DEBUG and final:
                for tau in range(NTAU):
                    em.dma("Y", dbg["h8"][tau], hsc[NE, tau], reads=(hsB[NE][tau],))
                    em.dma("Y", dbg["h0"][tau], hsc[0, tau], reads=(hsB[0][tau],))
            for tau in range(NTAU):
                xr_t, xr_B = (xr0, B["xr0"]) if tau % 2 == 0 else (xr1, B["xr1"])
                em.dma("G", xr_t[:], xs[tau], reads=(xsB[tau],), writes=(xr_B,))
                for e_i in range(NE + 1):
                    wd_t, wdB = (wed0, B["wed0"]) if e_i % 2 == 0 else (wed1, B["wed1"])
                    em.dma("Y", wd_t[:], mtd[e_i], reads=(mtB[e_i],), writes=(wdB,))
                    hi_t, hiB = (hh0, B["hh0"]) if e_i % 2 == 0 else (hh1, B["hh1"])
                    em.dma("G", hi_t[:], hsc[e_i, tau], reads=(hsB[e_i][tau],), writes=(hiB,))
                    for md in range(ND):
                        def dmm(e, wd_t=wd_t, hi_t=hi_t, md=md, first=(e_i == 0), last=(e_i == NE)):
                            ins = None
                            for fk in range(NMI):
                                ins = e.matmul(PB[md][:, :],
                                               wd_t[:, fk * 1024 + md * 128:fk * 1024 + (md + 1) * 128],
                                               hi_t[:, fk * TAU:(fk + 1) * TAU],
                                               start=(first and fk == 0), stop=(last and fk == NMI - 1))
                            return ins
                        em.op("T", dmm, reads=(wdB, hiB), writes=(pbf[md],))
                for md in range(ND):
                    em.op("V", (lambda e, md=md, xr_t=xr_t: e.tensor_tensor(
                        ub[:, md * TAU:(md + 1) * TAU], xr_t[:, md * TAU:(md + 1) * TAU],
                        PB[md][:, :], OP.add)),
                        reads=(pbf[md], xr_B), writes=(B["ub"],))
                if not final:
                    em.dma("G", xs[tau], ub[:], reads=(B["ub"],), writes=(xsB[tau],))
                else:
                    for cj in range(ND):
                        bk = [6, 7][cj % 2]

                        def tpf(e, cj=cj, bk=bk):
                            ins = None
                            for ti in range(4):
                                ins = e.transpose(PB[bk][:, ti * 128:(ti + 1) * 128],
                                                  ub[:, cj * TAU + ti * 128:cj * TAU + (ti + 1) * 128],
                                                  c_idf[:])
                            return ins
                        em.op("T", tpf, reads=(B["ub"], CB), writes=(pbf[bk],))
                        wtt, wtB = (wt0, B["wt0"]) if cj % 2 == 0 else (wt1, B["wt1"])
                        em.op("S", (lambda e, bk=bk, wtt=wtt: e.copy(wtt[:, 0:512], PB[bk][:, :])),
                              reads=(pbf[bk],), writes=(wtB,))
                        em.dma("G", y_out[tau * TAU:(tau + 1) * TAU, cj * 128:(cj + 1) * 128]
                               .rearrange("(a p) c -> p a c", p=128),
                               wtt[:, 0:512].rearrange("p (a c) -> p a c", a=4),
                               reads=(wtB,), writes=(yB,))

        def attn_b_prep(i):
            for wi, (src, dstt, dbuf) in enumerate(((dt["bWq"], qscr, scrB["q"]),
                                                    (dt["bWk"], qscr, scrB["q"]),
                                                    (dt["bWv"], qscr, scrB["q"]),
                                                    (dt["bWo"], oscr, scrB["o"]))):
                for dj in range(ND):
                    em.dma("Y", wt0[:, 0:1024], src[i, dj * 128:(dj + 1) * 128, :], writes=(B["wt0"],))
                    em.op("S", lambda e: e.copy(wt1b[:, 0:1024], wt0[:, 0:1024]),
                          reads=(B["wt0"],), writes=(B["wt1"],))
                    if wi < 3:
                        dst = dstt[i, wi, :, dj].rearrange("m p c -> p m c")
                    else:
                        dst = dstt[i, :, dj].rearrange("m p c -> p m c")
                    em.dma("G", dst, wt1b[:, 0:1024].rearrange("p (m c) -> p m c", m=8),
                           reads=(B["wt1"],), writes=(dbuf[i],))
            for dj in range(ND):
                for hh in range(2):
                    em.dma("Y", wt0[:, 0:2048],
                           dt["bW1"][i, dj * 128:(dj + 1) * 128, hh * 2048:(hh + 1) * 2048],
                           writes=(B["wt0"],))
                    em.op("S", lambda e: e.copy(wt1b[:, 0:2048], wt0[:, 0:2048]),
                          reads=(B["wt0"],), writes=(B["wt1"],))
                    dst = w1scr[i, hh * 16:(hh + 1) * 16, dj].rearrange("m p c -> p m c")
                    em.dma("G", dst, wt1b[:, 0:2048].rearrange("p (m c) -> p m c", m=16),
                           reads=(B["wt1"],), writes=(scrB["w1"][i],))
            for fk in range(NFF):
                em.dma("Y", wt0[:, 0:1024], dt["bW2"][i, fk * 128:(fk + 1) * 128, :], writes=(B["wt0"],))
                em.op("S", lambda e: e.copy(wt1b[:, 0:1024], wt0[:, 0:1024]),
                      reads=(B["wt0"],), writes=(B["wt1"],))
                dst = w2scr[i, :, fk].rearrange("m p c -> p m c")
                em.dma("G", dst, wt1b[:, 0:1024].rearrange("p (m c) -> p m c", m=8),
                       reads=(B["wt1"],), writes=(scrB["w2"][i],))

        def attn_b_params(i):
            col_param(dt["bbq"][i], ND, pcol2, B["pcol2"], 0, 4)
            col_param(dt["bbk"][i], ND, pcol2, B["pcol2"], 8, 4)
            col_param(dt["bbv"][i], ND, pcol2, B["pcol2"], 16, 4)
            col_param(dt["bbo"][i], ND, pcol2, B["pcol2"], 24, 4)
            col_param(dt["bb2"][i], ND, pcol2, B["pcol2"], 32, 4)
            col_param(dt["bln1s"][i], ND, pcol2, B["pcol2"], 40, 4)
            col_param(dt["bln1b"][i], ND, pcol2, B["pcol2"], 48, 4)
            col_param(dt["bln2s"][i], ND, pcol2, B["pcol2"], 56, 4)
            col_param(dt["bln2b"][i], ND, pcol2, B["pcol2"], 64, 4)
            col_param(dt["bb1"][i], 32, pcol2, B["pcol2"], 72, 4)

        def attn_b_block(i, dep0=()):
            attn_b_params(i)
            em.op("V", lambda e: e.memset(kvacc[:], 0.0), writes=(B["kvacc"],))
            em.op("V", lambda e: e.memset(ksacc[:], 0.0), writes=(B["ksacc"],))
            wsl = [(wsq0, B["wsq0"]), (wsq1, B["wsq1"]), (wsq2, B["wsq2"]), (wsq3, B["wsq3"])]
            for tau in range(NTAU):
                xr_t, xr_B = (xr0, B["xr0"]) if tau % 2 == 0 else (xr1, B["xr1"])
                em.dma("G", xr_t[:], xbs[tau], reads=(xbB[tau],), writes=(xr_B,))
                em.op("S", (lambda e, xr_t=xr_t: e.copy(lb1[:], xr_t[:])),
                      reads=(xr_B,), writes=(B["lb1"],))
                slot = 0
                for wi in range(3):
                    for m in range(ND):
                        ws_t, wsB = wsl[slot % 4]
                        slot += 1
                        em.dma("Y", ws_t[:].rearrange("p (k c) -> p k c", k=8),
                               qscr[i, wi, m].rearrange("k p c -> p k c"),
                               reads=(scrB["q"][i],), writes=(wsB,),
                               deps=(dep0 if (tau == 0 and wi == 0 and m == 0) else ()))
                        bk = [0, 1][m % 2]

                        def pf(e, ws_t=ws_t, bk=bk):
                            ins = None
                            for k in range(ND):
                                ins = e.matmul(PB[bk][:, :], ws_t[:, k * 128:(k + 1) * 128],
                                               lb1[:, k * TAU:(k + 1) * TAU],
                                               start=(k == 0), stop=(k == ND - 1))
                            return ins
                        em.op("T", pf, reads=(wsB, B["lb1"]), writes=(pbf[bk],))
                        if wi == 0:
                            def oq(m=m):
                                em.op("V", (lambda e, m=m: e.tensor_tensor(
                                    lb0[:, m * TAU:(m + 1) * TAU], sc_[:], sd_[:], OP.add)),
                                    reads=(B["sc_"], B["sd_"]), writes=(B["lb0"],))
                            elu1(bk, pcol2, B["pcol2"], 0 + m, oq)
                        elif wi == 1:
                            def ok(m=m):
                                em.op("V", (lambda e, m=m: e.tensor_tensor(
                                    ua[:, m * TAU:(m + 1) * TAU], sc_[:], sd_[:], OP.add)),
                                    reads=(B["sc_"], B["sd_"]), writes=(B["ua"],))
                            elu1(bk, pcol2, B["pcol2"], 8 + m, ok)
                        else:
                            em.op("V", (lambda e, m=m, bk=bk: e.tensor_scalar(
                                ub[:, m * TAU:(m + 1) * TAU], PB[bk][:, :],
                                pcol2[:, 16 + m:16 + m + 1], None, OP.add)),
                                reads=(pbf[bk], B["pcol2"]), writes=(B["ub"],))
                            em.op("V", (lambda e, m=m: e.tensor_tensor(
                                se_[:], ua[:, m * TAU:(m + 1) * TAU],
                                ub[:, m * TAU:(m + 1) * TAU], OP.mult)),
                                reads=(B["ua"], B["ub"]), writes=(B["se_"],))
                            em.op("V", (lambda e, m=m, tau=tau: e.tensor_reduce(
                                kvacc[:, m * (NTAU + 1) + tau:m * (NTAU + 1) + tau + 1]
                                .rearrange("p (a b) -> p a b", a=1),
                                se_[:].rearrange("p (a t) -> p a t", a=1), AX.X, OP.add)),
                                reads=(B["se_"],), writes=(B["kvacc"],))
                            em.op("V", (lambda e, m=m, tau=tau: e.tensor_reduce(
                                ksacc[:, m * (NTAU + 1) + tau:m * (NTAU + 1) + tau + 1]
                                .rearrange("p (a b) -> p a b", a=1),
                                ua[:, m * TAU:(m + 1) * TAU].rearrange("p (a t) -> p a t", a=1),
                                AX.X, OP.add)),
                                reads=(B["ua"],), writes=(B["ksacc"],))
                em.dma("G", qsc[tau], lb0[:], reads=(B["lb0"],), writes=(qsB[tau],))
            for m in range(ND):
                em.op("V", (lambda e, m=m: e.tensor_reduce(
                    kvacc[:, m * (NTAU + 1) + NTAU:m * (NTAU + 1) + NTAU + 1]
                    .rearrange("p (a b) -> p a b", a=1),
                    kvacc[:, m * (NTAU + 1):m * (NTAU + 1) + NTAU]
                    .rearrange("p (a t) -> p a t", a=1), AX.X, OP.add)),
                    reads=(B["kvacc"],), writes=(B["kvacc"],))
                em.op("V", (lambda e, m=m: e.tensor_reduce(
                    ksacc[:, m * (NTAU + 1) + NTAU:m * (NTAU + 1) + NTAU + 1]
                    .rearrange("p (a b) -> p a b", a=1),
                    ksacc[:, m * (NTAU + 1):m * (NTAU + 1) + NTAU]
                    .rearrange("p (a t) -> p a t", a=1), AX.X, OP.add)),
                    reads=(B["ksacc"],), writes=(B["ksacc"],))
            for tau in range(NTAU):
                xr_t, xr_B = (xr0, B["xr0"]) if tau % 2 == 0 else (xr1, B["xr1"])
                em.dma("G", lb0[:], qsc[tau], reads=(qsB[tau],), writes=(B["lb0"],))
                em.dma("G", xr_t[:], xbs[tau], reads=(xbB[tau],), writes=(xr_B,))
                for m in range(ND):
                    kvcol = kvacc[:, m * (NTAU + 1) + NTAU:m * (NTAU + 1) + NTAU + 1]
                    kscol = ksacc[:, m * (NTAU + 1) + NTAU:m * (NTAU + 1) + NTAU + 1]
                    em.op("V", (lambda e, m=m, kscol=kscol: e.tensor_scalar(
                        sa[:], lb0[:, m * TAU:(m + 1) * TAU], kscol, 1e-6, OP.mult, OP.add)),
                        reads=(B["lb0"], B["ksacc"]), writes=(B["sa"],))
                    em.op("S", lambda e: e.activation(sc_[:], sa[:], AF.Ln),
                          reads=(B["sa"],), writes=(B["sc_"],))
                    em.op("S", lambda e: e.activation(sd_[:], sc_[:], AF.Exp, scale=-1.0),
                          reads=(B["sc_"],), writes=(B["sd_"],))
                    em.op("V", (lambda e, m=m, kvcol=kvcol: e.scalar_tensor_tensor(
                        lb3[:, m * TAU:(m + 1) * TAU], lb0[:, m * TAU:(m + 1) * TAU], kvcol,
                        sd_[:], OP.mult, OP.mult)),
                        reads=(B["lb0"], B["kvacc"], B["sd_"]), writes=(B["lb3"],))
                wsl2 = [(wsq0, B["wsq0"]), (wsq1, B["wsq1"]), (wsq2, B["wsq2"]), (wsq3, B["wsq3"])]
                for m in range(ND):
                    ws_t, wsB = wsl2[m % 4]
                    em.dma("Y", ws_t[:].rearrange("p (k c) -> p k c", k=8),
                           oscr[i, m].rearrange("k p c -> p k c"),
                           reads=(scrB["o"][i],), writes=(wsB,))
                    bk = [0, 1][m % 2]

                    def wof(e, ws_t=ws_t, bk=bk):
                        ins = None
                        for k in range(ND):
                            ins = e.matmul(PB[bk][:, :], ws_t[:, k * 128:(k + 1) * 128],
                                           lb3[:, k * TAU:(k + 1) * TAU],
                                           start=(k == 0), stop=(k == ND - 1))
                        return ins
                    em.op("T", wof, reads=(wsB, B["lb3"]), writes=(pbf[bk],))
                    em.op("S", (lambda e, m=m, bk=bk: e.activation(
                        sa[:], PB[bk][:, :], AF.Gelu_apprx_tanh, bias=pcol2[:, 24 + m:24 + m + 1])),
                        reads=(pbf[bk], B["pcol2"]), writes=(B["sa"],))
                    em.op("S", lambda e: e.activation(sc_[:], sa[:], AF.Gelu_apprx_tanh),
                          reads=(B["sa"],), writes=(B["sc_"],))
                    em.op("V", (lambda e, m=m, xr_t=xr_t: e.tensor_tensor(
                        ua[:, m * TAU:(m + 1) * TAU], sc_[:], xr_t[:, m * TAU:(m + 1) * TAU], OP.add)),
                        reads=(B["sc_"], xr_B), writes=(B["ua"],))
                inner_ln(ua, B["ua"], (pcol2, B["pcol2"], 40, 48), out_f32=(xr_t, xr_B),
                         out_bf=(lb1, B["lb1"]))
                for m in range(NFF):
                    ws_t, wsB = wsl2[m % 4]
                    em.dma("Y", ws_t[:].rearrange("p (k c) -> p k c", k=8),
                           w1scr[i, m].rearrange("k p c -> p k c"),
                           reads=(scrB["w1"][i],), writes=(wsB,))
                    bk = [0, 1][m % 2]

                    def w1f(e, ws_t=ws_t, bk=bk):
                        ins = None
                        for k in range(ND):
                            ins = e.matmul(PB[bk][:, :], ws_t[:, k * 128:(k + 1) * 128],
                                           lb1[:, k * TAU:(k + 1) * TAU],
                                           start=(k == 0), stop=(k == ND - 1))
                        return ins
                    em.op("T", w1f, reads=(wsB, B["lb1"]), writes=(pbf[bk],))
                    em.op("S", (lambda e, m=m, bk=bk: e.activation(
                        hfb[:, m * TAU:(m + 1) * TAU], PB[bk][:, :], AF.Gelu_apprx_tanh,
                        bias=pcol2[:, 72 + m:72 + m + 1])),
                        reads=(pbf[bk], B["pcol2"]), writes=(B["hfb"],))
                wsl3 = [(wsw0, B["wsw0"]), (wsw1, B["wsw1"])]
                for md in range(ND):
                    ws_t, wsB = wsl3[md % 2]
                    em.dma("Y", ws_t[:].rearrange("p (k c) -> p k c", k=32),
                           w2scr[i, md].rearrange("k p c -> p k c"),
                           reads=(scrB["w2"][i],), writes=(wsB,))
                    bk = [2, 3][md % 2]

                    def w2f(e, ws_t=ws_t, bk=bk):
                        ins = None
                        for k in range(NFF):
                            ins = e.matmul(PB[bk][:, :], ws_t[:, k * 128:(k + 1) * 128],
                                           hfb[:, k * TAU:(k + 1) * TAU],
                                           start=(k == 0), stop=(k == NFF - 1))
                        return ins
                    em.op("T", w2f, reads=(wsB, B["hfb"]), writes=(pbf[bk],))
                    em.op("S", (lambda e, md=md, bk=bk: e.activation(
                        sa[:], PB[bk][:, :], AF.Gelu_apprx_tanh, bias=pcol2[:, 32 + md:32 + md + 1])),
                        reads=(pbf[bk], B["pcol2"]), writes=(B["sa"],))
                    em.op("V", (lambda e, md=md, xr_t=xr_t: e.tensor_tensor(
                        ub[:, md * TAU:(md + 1) * TAU], sa[:], xr_t[:, md * TAU:(md + 1) * TAU],
                        OP.add)),
                        reads=(B["sa"], xr_B), writes=(B["ub"],))
                inner_ln2(ub, B["ub"], (pcol2, B["pcol2"], 56, 64), out_f32=(xr_t, xr_B))
                if i < 3:
                    em.dma("G", xbs[tau], xr_t[:], reads=(xr_B,), writes=(xbB[tau],))
                else:
                    xo_t, xo_B = (xr1, B["xr1"]) if tau % 2 == 0 else (xr0, B["xr0"])
                    em.dma("G", xo_t[:], xs[tau], reads=(xsB[tau],), writes=(xo_B,))
                    em.op("V", lambda e, xr_t=xr_t, xo_t=xo_t: e.tensor_tensor(
                        ua[:], xr_t[:], xo_t[:], OP.add),
                        reads=(xo_B, xr_B), writes=(B["ua"],))
                    em.dma("G", xs[tau], ua[:], reads=(B["ua"],), writes=(xsB[tau],))

        def inner_ln(src_t, src_B, aff, out_f32, out_bf):
            s_t, sB, scol, bcol = aff
            em.op("S", lambda e: e.copy(lb3[:], src_t[:]), reads=(src_B,), writes=(B["lb3"],))
            em.op("V", lambda e: e.tensor_tensor(lb0[:], lb3[:], lb3[:], OP.mult),
                  reads=(B["lb3"],), writes=(B["lb0"],))

            def statf(e, srcv, bk):
                ins = None
                for j in range(ND):
                    ins = e.matmul(PB[bk][0:1, :], c_ones_col[:], srcv[:, j * TAU:(j + 1) * TAU],
                                   start=(j == 0), stop=(j == ND - 1))
                return ins
            em.op("T", lambda e: statf(e, lb3, 6), reads=(B["lb3"], CB), writes=(pbf[6],))
            em.op("T", lambda e: statf(e, lb0, 7), reads=(B["lb0"], CB), writes=(pbf[7],))
            em.op("V", lambda e: e.tensor_scalar(rows[0:1, 0:TAU], PB[6][0:1, :], 1.0 / D, None, OP.mult),
                  reads=(pbf[6],), writes=(B["rows"],))
            em.op("V", lambda e: e.tensor_tensor(rows2[0:1, TAU:2 * TAU], rows[0:1, 0:TAU],
                                                 rows[0:1, 0:TAU], OP.mult),
                  reads=(B["rows"],), writes=(B["rows2"],))
            em.op("V", lambda e: e.scalar_tensor_tensor(rows2[0:1, 0:TAU], PB[7][0:1, :],
                                                        1.0 / D, rows2[0:1, TAU:2 * TAU],
                                                        OP.mult, OP.subtract),
                  reads=(pbf[7], B["rows2"]), writes=(B["rows2"],))
            em.op("V", lambda e: e.tensor_scalar(rows[0:1, TAU:2 * TAU], rows2[0:1, 0:TAU],
                                                 float(EPS_IN), None, OP.add),
                  reads=(B["rows2"],), writes=(B["rows"],))
            em.op("S", lambda e: e.activation(rows2[0:1, TAU:2 * TAU], rows[0:1, TAU:2 * TAU],
                                              AF.Ln),
                  reads=(B["rows"],), writes=(B["rows2"],))
            em.op("S", lambda e: e.activation(rows[0:1, TAU:2 * TAU], rows2[0:1, TAU:2 * TAU],
                                              AF.Exp, scale=-0.5),
                  reads=(B["rows2"],), writes=(B["rows"],))
            em.op("V", lambda e: e.scalar_tensor_tensor(rows2[0:1, 0:TAU], rows[0:1, 0:TAU],
                                                        -1.0, rows[0:1, TAU:2 * TAU], OP.mult, OP.mult),
                  reads=(B["rows"], B["rows2"]), writes=(B["rows2"],))
            em.op("T", mm(PB[6][:, :], c_ones_row[:], rows[0:1, TAU:2 * TAU], True, True),
                  reads=(B["rows"], CB), writes=(pbf[6],))
            em.op("T", mm(PB[7][:, :], c_ones_row[:], rows2[0:1, 0:TAU], True, True),
                  reads=(B["rows2"], CB), writes=(pbf[7],))
            for j in range(ND):
                em.op("V", (lambda e, j=j: e.tensor_tensor(
                    sa[:], src_t[:, j * TAU:(j + 1) * TAU], PB[6][:, :], OP.mult)),
                    reads=(src_B, pbf[6]), writes=(B["sa"],))
                em.op("V", (lambda e, j=j: e.tensor_tensor(sc_[:], sa[:], PB[7][:, :], OP.add)),
                      reads=(B["sa"], pbf[7]), writes=(B["sc_"],))
                em.op("V", (lambda e, j=j: e.tensor_scalar(
                    out_f32[0][:, j * TAU:(j + 1) * TAU], sc_[:],
                    s_t[:, scol + j:scol + j + 1], s_t[:, bcol + j:bcol + j + 1], OP.mult, OP.add)),
                    reads=(B["sc_"], sB), writes=(out_f32[1],))
            if out_bf is not None:
                em.op("S", lambda e: e.copy(out_bf[0][:], out_f32[0][:]),
                      reads=(out_f32[1],), writes=(out_bf[1],))

        inner_ln2 = lambda src_t, src_B, aff, out_f32: inner_ln(src_t, src_B, aff, out_f32, None)

        for s in range(NSEQ):
            for tl in range(4):
                attn_a_pass1(s, tl)
            attn_a_finalize(s)
            if s == 0:
                moe_prep()
        def wot_dst(ri, cj):
            col = cj * 1024 + ri * 128
            if col < 4096:
                return ktT[:, col:col + 128], B["ktT"]
            return vtT[:, col - 4096:col - 4096 + 128], B["vtT"]
        prep_T(wot_dst, dt["aWo"], ND, ND, [0, 1])
        for s in range(NSEQ):
            for tl in range(4):
                attn_a_phase3(s, tl)

        def dump(k, src, srcB):
            if DEBUG:
                for tau in range(NTAU):
                    em.dma("Y", dbg[k][tau], src[tau], reads=(srcB[tau],))

        dump(0, xs, xsB)
        b_moeA = barrier()
        moe_phase(final=False, dep0=b_moeA)
        dump(1, xs, xsB)
        for tau in range(NTAU):
            xr_t, xr_B = (xr0, B["xr0"]) if tau % 2 == 0 else (xr1, B["xr1"])
            em.dma("G", xr_t[:], xs[tau], reads=(xsB[tau],), writes=(xr_B,))
            ln_full(xr_t, xr_B, EPS_OUT, out_f32=(ua, B["ua"]))
            em.dma("G", xbs[tau], ua[:], reads=(B["ua"],), writes=(xbB[tau],))
        dump(2, xbs, xbB)
        attn_b_prep(0)
        b_attnb = barrier()
        for i in range(4):
            attn_b_block(i, dep0=(b_attnb if i == 0 else ()))
            if i < 3:
                attn_b_prep(i + 1)
        dump(3, xs, xsB)
        b_moeB = barrier()
        moe_phase(final=True, dep0=b_moeB)

        gfin = em.cnt["G"]
        yfin = em.cnt["Y"]
        em.items["Y"].append(("w", "G", gfin))
        em.items["G"].append(("w", "Y", yfin))

        @block.tensor
        def _(eng):
            em.replay("T", eng, sems)

        @block.vector
        def _(eng):
            em.replay("V", eng, sems)

        @block.scalar
        def _(eng):
            em.replay("S", eng, sems)

        @block.gpsimd
        def _(eng):
            em.replay("G", eng, sems)

        @block.sync
        def _(eng):
            em.replay("Y", eng, sems)

    _EM = em
    return nc


_BF = ml_dtypes.bfloat16


def _consts():
    idf = np.eye(128, dtype=np.float32)
    ones_col = np.ones((128, 1), dtype=np.float32).astype(_BF)
    ones_row = np.ones((1, 128), dtype=np.float32)
    sel16 = np.zeros((16, H * HD), dtype=np.float32)
    for h in range(16):
        sel16[h, h * 64:(h + 1) * 64] = 1.0
    sel8 = np.zeros((8, NE * 128), dtype=np.float32)
    for e in range(8):
        sel8[e, e * 128:(e + 1) * 128] = 1.0
    return dict(idf=idf, ones_col=ones_col, ones_row=ones_row,
                sel16=sel16.astype(_BF), sel8=sel8.astype(_BF))


def kernel(**inp):
    x = np.ascontiguousarray(inp["L0_attn_a_x"])
    consts = _consts()
    common = dict(
        aWq=inp["L0_attn_a_Wq"], aWk=inp["L0_attn_a_Wk"], aWv=inp["L0_attn_a_Wv"],
        aWo=inp["L0_attn_a_Wo"], abq=inp["L0_attn_a_bq"], abk=inp["L0_attn_a_bk"],
        abv=inp["L0_attn_a_bv"], abo=inp["L0_attn_a_bo"],
        bWq=inp["L0_attn_b_Wq"], bWk=inp["L0_attn_b_Wk"], bWv=inp["L0_attn_b_Wv"],
        bWo=inp["L0_attn_b_Wo"], bbq=inp["L0_attn_b_bq"], bbk=inp["L0_attn_b_bk"],
        bbv=inp["L0_attn_b_bv"], bbo=inp["L0_attn_b_bo"],
        bln1s=inp["L0_attn_b_ln1_s"], bln1b=inp["L0_attn_b_ln1_b"],
        bln2s=inp["L0_attn_b_ln2_s"], bln2b=inp["L0_attn_b_ln2_b"],
        bW1=inp["L0_attn_b_W1"], bb1=inp["L0_attn_b_b1"],
        bW2=inp["L0_attn_b_W2"], bb2=inp["L0_attn_b_b2"],
        ggw=inp["L0_ffn_gate_w"], gwg=inp["L0_ffn_wg"], gwu=inp["L0_ffn_wu"],
        gwd=inp["L0_ffn_wd"], gsg=inp["L0_ffn_sg"], gsu=inp["L0_ffn_su"],
        gsd=inp["L0_ffn_sd"],
    )
    common = {k: np.ascontiguousarray(np.asarray(v, dtype=np.float32)) for k, v in common.items()}
    common.update(consts)
    nc = build_program()
    in_maps = []
    for c in range(8):
        m = dict(common)
        m["x"] = np.ascontiguousarray(x[2 * c:2 * c + 2].reshape(NTOK, D))
        in_maps.append(m)
    res = run_bass_kernel_spmd(nc, in_maps, core_ids=list(range(8)))
    out = np.stack([r["y"].reshape(2, T, D) for r in res.results], axis=0)
    return np.ascontiguousarray(out.reshape(16, T, D).astype(np.float32))


if __name__ == "__main__":
    nc = build_program()
    print("build ok")
